# version 03 (final)
# baseline (speedup 1.0000x reference)
import numpy as np
import ml_dtypes
import concourse.bass as bass
import concourse.mybir as mybir
from concourse.bass_utils import run_bass_kernel_spmd

F32 = mybir.dt.float32
BF16 = mybir.dt.bfloat16
AF = mybir.ActivationFunctionType
ALU = mybir.AluOpType
AX = mybir.AxisListType

NCORES = 8
T = 1024
L = 8192
D = 4096
I = 11008
NH = 16
HD = 256
DC = D // 128
ICn = I // 128
LT = T // 128
EPS = 1e-5

def _build_tables(g):
    dim = len(g)
    n = 2 ** dim
    masks = sorted(range(n), key=lambda m: (bin(m).count('1'), m))
    inv = {m: i for i, m in enumerate(masks)}
    C = np.zeros((n, n, n), dtype=np.float64)
    for i, a in enumerate(masks):
        for j, b in enumerate(masks):
            sw, at = (0, a >> 1)
            while at:
                sw += bin(at & b).count('1')
                at >>= 1
            s = -1.0 if sw & 1 else 1.0
            for t in range(dim):
                if a >> t & 1 and b >> t & 1:
                    s *= float(g[t])
            C[i, j, inv[a ^ b]] = s
    grades = np.array([bin(m).count('1') for m in masks])
    rev = np.where(grades * (grades - 1) // 2 % 2 == 1, -1.0, 1.0)
    met = np.array([float(np.prod([g[t] for t in range(dim) if m >> t & 1] + [1.0])) for m in masks])
    return C, rev, met

def _host_constants():
    C, rev, met = _build_tables([1.0, 1.0, 1.0])
    BIV = [4, 5, 6]
    scale = 1.0 / np.sqrt(HD)
    biv = np.zeros((3, 128, 128), np.float64)
    sblk = np.zeros((128, 128), np.float64)
    for c in range(3):
        blk = rev[:, None] * C[:, :, BIV[c]]
        for cc in range(16):
            biv[c, cc * 8:(cc + 1) * 8, cc * 8:(cc + 1) * 8] = blk
    dblk = np.diag(rev * met * scale)
    for cc in range(16):
        sblk[cc * 8:(cc + 1) * 8, cc * 8:(cc + 1) * 8] = dblk
    ind = np.zeros((16, 128, 16), np.float32)
    for v in range(16):
        ind[v, :, v] = 1.0
    return {
        "c_biv": np.ascontiguousarray(biv.reshape(384, 128)).astype(np.float32),
        "c_sblk": sblk.astype(np.float32),
        "c_idn": np.eye(128, dtype=np.float32),
        "c_ind": np.ascontiguousarray(ind.reshape(16 * 128, 16)).astype(ml_dtypes.bfloat16),
    }


class _Book:
    def __init__(self):
        self.cnt = {}
    def inc(self, name, amt=1):
        self.cnt[name] = self.cnt.get(name, 0) + amt
        return self.cnt[name]
    def get(self, name):
        return self.cnt.get(name, 0)


class DmaPool:
    def __init__(self, nc, prefix, n=8):
        self.sems = [nc.semaphore(f"{prefix}{i}").__enter__() for i in range(n)]
        self.n = n
        self.cnt = [0] * n
        self.k = 0
        self.marks = {}
    def guard(self, eng):
        i = self.k % self.n
        if self.cnt[i]:
            eng.wait_ge(self.sems[i], self.cnt[i])
    def attach(self, instr, tag=None):
        i = self.k % self.n
        self.k += 1
        self.cnt[i] += 16
        instr.then_inc(self.sems[i], 16)
        if tag is not None:
            assert tag not in self.marks, tag
            self.marks[tag] = (i, self.cnt[i])
    def snap(self, tag):
        assert tag not in self.marks, tag
        self.marks[tag] = ("ALL", list(self.cnt))
    def wait(self, eng, tag):
        i, v = self.marks[tag]
        if i == "ALL":
            for j, vj in enumerate(v):
                if vj:
                    eng.wait_ge(self.sems[j], vj)
        else:
            eng.wait_ge(self.sems[i], v)


def build_nc(upto="full", cc=NCORES):
    nc = bass.Bass("TRN2", target_bir_lowering=False, detect_race_conditions=False)

    x_in = nc.dram_tensor("x", [T, D], F32, kind="ExternalInput")
    w_q = nc.dram_tensor("Wq", [D, D], F32, kind="ExternalInput")
    w_k = nc.dram_tensor("Wk", [D, D], F32, kind="ExternalInput")
    w_v = nc.dram_tensor("Wv", [D, D], F32, kind="ExternalInput")
    b_q = nc.dram_tensor("bq", [D], F32, kind="ExternalInput")
    b_k = nc.dram_tensor("bk", [D], F32, kind="ExternalInput")
    b_v = nc.dram_tensor("bv", [D], F32, kind="ExternalInput")
    bw_in = nc.dram_tensor("bw", [1, NH * 3], F32, kind="ExternalInput")
    w_g = nc.dram_tensor("Wg", [I, D], F32, kind="ExternalInput")
    w_u = nc.dram_tensor("Wu", [I, D], F32, kind="ExternalInput")
    w_d = nc.dram_tensor("Wd", [D, I], F32, kind="ExternalInput")
    ag_in = nc.dram_tensor("ag", [I], F32, kind="ExternalInput")
    au_in = nc.dram_tensor("au", [I], F32, kind="ExternalInput")
    ad_in = nc.dram_tensor("ad", [D], F32, kind="ExternalInput")
    c_biv = nc.dram_tensor("c_biv", [384, 128], F32, kind="ExternalInput")
    c_sblk = nc.dram_tensor("c_sblk", [128, 128], F32, kind="ExternalInput")
    c_idn = nc.dram_tensor("c_idn", [128, 128], F32, kind="ExternalInput")
    c_ind = nc.dram_tensor("c_ind", [16 * 128, 16], BF16, kind="ExternalInput")
    out = nc.dram_tensor("out", [T, D], F32, kind="ExternalOutput")
    dbgx = nc.dram_tensor("dbgx", [DC * 128, T], F32, kind="ExternalOutput") if upto.endswith("raw") else None
    dbgz = nc.dram_tensor("dbgz", [128, 32], F32, kind="ExternalOutput") if upto.endswith("raw") else None

    wg_b = nc.dram_tensor("wg_b", [I, D], BF16)
    wu_b = nc.dram_tensor("wu_b", [I, D], BF16)
    wd_b = nc.dram_tensor("wd_b", [D, I], BF16)
    wq_b = nc.dram_tensor("wq_b", [D, D], BF16)
    wk_b = nc.dram_tensor("wk_b", [D, D], BF16)
    wv_b = nc.dram_tensor("wv_b", [D, D], BF16)
    xres = nc.dram_tensor("xres", [DC, 128, T], F32)
    hT_d = nc.dram_tensor("hT_d", [2, ICn, 128, 512], BF16)
    kin_k = nc.dram_tensor("kin_k", [D, T], BF16)
    kin_v = nc.dram_tensor("kin_v", [D, T], BF16)
    kg = nc.dram_tensor("kg", [cc * D, T], BF16)
    vg = nc.dram_tensor("vg", [cc * D, T], BF16)
    k2_in = nc.dram_tensor("k2_in", [NH, T], F32)
    k2_g = nc.dram_tensor("k2_g", [NH, T], F32)

    r_ln = nc.alloc_sbuf_tensor("r_ln", [128, 32768], BF16)
    r_big = nc.alloc_sbuf_tensor("r_big", [128, 45056], BF16)
    r_big32 = r_big.bitcast(F32)

    m_xm = nc.alloc_sbuf_tensor("m_xm", [128, 8], F32)
    m_rs = nc.alloc_sbuf_tensor("m_rs", [128, 8], F32)
    m_sc = nc.alloc_sbuf_tensor("m_sc", [128, 8], F32)
    m_h = nc.alloc_sbuf_tensor("m_h", [128, 2048], BF16)
    m_xsl = nc.alloc_sbuf_tensor("m_xsl", [128, 2048], F32)
    m_xnew = nc.alloc_sbuf_tensor("m_xnew", [128, 2048], F32)
    m_xb16 = nc.alloc_sbuf_tensor("m_xb16", [128, 2048], BF16)
    m_sq = nc.alloc_sbuf_tensor("m_sq", [128, 2048], BF16)
    m_qt = nc.alloc_sbuf_tensor("m_qt", [128, 2048], BF16)
    m_zs = nc.alloc_sbuf_tensor("m_zs", [NH, 8], F32)
    m_negz = nc.alloc_sbuf_tensor("m_negz", [128, NH], F32)
    m_vt = nc.alloc_sbuf_tensor("m_vt", [128, 1024], BF16)
    m_pt = nc.alloc_sbuf_tensor("m_pt", [128, 2048], BF16)
    m_rinv = nc.alloc_sbuf_tensor("m_rinv", [128, 8], F32)

    m_lntm = r_big[:, 32768:36864]
    m_xstage = r_big32[:, 12288:16384]
    m_tsg = r_big32[:, 8192:9216]
    m_ab0 = r_big32[:, 0:1024]
    m_ab1 = r_big32[:, 1024:2048]
    m_sum = r_big32[:, 16384:17408]
    m_tree = r_big32[:, 17408:18432]
    m_xup = m_tree
    m_rb = r_big32[:, 18432:19456]
    m_vecs = r_big32[:, 20480:21248]
    m_srow = r_big32[0:1, 19456:20480]
    m_srow2 = r_big32[0:1, 20480:21504]
    m_rrow = r_big32[0:1, 21504:22528]
    m_k2 = r_big32[0:NH, 16384:17408]
    m_q2 = r_big32[0:NH, 17408:18432]

    c_idn_s = nc.alloc_sbuf_tensor("c_idn_s", [128, 128], F32)
    c_biv_s = nc.alloc_sbuf_tensor("c_biv_s", [128, 3 * 128], F32)
    c_sblk_s = nc.alloc_sbuf_tensor("c_sblk_s", [128, 128], F32)
    c_ind_s = nc.alloc_sbuf_tensor("c_ind_s", [128, 256], BF16)
    c_ones1 = nc.alloc_sbuf_tensor("c_ones1", [1, 128], F32)
    c_misc = nc.alloc_sbuf_tensor("c_misc", [128, 4], F32)
    c_onesc = nc.alloc_sbuf_tensor("c_onesc", [128, 1], BF16)
    m_bwB = nc.alloc_sbuf_tensor("m_bwB", [128, 48], F32)
    m_bmx = nc.alloc_sbuf_tensor("m_bmx", [128, 16 * 128], BF16)
    m_bacc = nc.alloc_sbuf_tensor("m_bacc", [128, 256], F32)
    m_agF = nc.alloc_sbuf_tensor("m_agF", [128, ICn], F32)
    m_auF = nc.alloc_sbuf_tensor("m_auF", [128, ICn], F32)
    m_adh = nc.alloc_sbuf_tensor("m_adh", [128, DC], F32)
    m_bF = nc.alloc_sbuf_tensor("m_bF", [128, 3 * DC], F32)

    def _buf(t, w):
        def f(par):
            return t[:, par * w:(par + 1) * w]
        return f
    hbuf = _buf(m_h, 1024); xslb = _buf(m_xsl, 1024); xnb = _buf(m_xnew, 1024)
    xbb = _buf(m_xb16, 1024); sqb = _buf(m_sq, 1024); qtb = _buf(m_qt, 1024)
    vtb = _buf(m_vt, 256); ptb = _buf(m_pt, 1024); bacb = _buf(m_bacc, 128)

    PA = nc.alloc_psum_tensor("PA", [128, 1024], F32)
    PB = nc.alloc_psum_tensor("PB", [128, 1024], F32)
    PCp = nc.alloc_psum_tensor("PCp", [128, 1024], F32)
    PD = nc.alloc_psum_tensor("PD", [128, 1024], F32)

    def lnT_v(dc, t0, tn):
        return r_ln[:, dc * 1024 + t0: dc * 1024 + t0 + tn]
    def lnT_tr_out(lt):
        return r_ln[:, 0:32768].rearrange("p (c j) -> p c j", c=DC)[:, :, lt * 128:(lt + 1) * 128]
    def kh_v(par, cd, c0, cn):
        return r_ln[:, par * 16384 + cd * 8192 + c0: par * 16384 + cd * 8192 + c0 + cn]
    def wdts_v(par, c0, cn):
        return r_ln[:, par * 11008 + c0: par * 11008 + c0 + cn]
    def hT_v(ic, t0, tn):
        return r_big[:, ic * 512 + t0: ic * 512 + t0 + tn]
    def qeff_v(dmc, t0, tn):
        return r_big[:, dmc * 1024 + t0: dmc * 1024 + t0 + tn]
    def gusl_v(w, par, c0, cn):
        return r_big[:, (w * 2 + par) * 4096 + c0: (w * 2 + par) * 4096 + c0 + cn]
    def pjsl_v(par, c0, cn):
        return r_big[:, 36864 + par * 4096 + c0: 36864 + par * 4096 + c0 + cn]
    def ph0_x(par):
        return r_big32[:, par * 4096:(par + 1) * 4096]
    ph0_sq = r_big32[:, 8192:12288]

    B = _Book()
    SP = DmaPool(nc, "spool")
    GP = DmaPool(nc, "gpool")
    sem_names = [
        "sVC", "sSC",
        "sSQ", "sVAR", "sRST", "sLNA", "sPE0", "sEV0", "sBM",
        "sGU", "sSI", "sH", "sDN", "sXN",
        "sXB", "sQQ", "sST", "sLB",
        "sPJ", "sEVJ", "sMX", "sCCk", "sCCv", "sCC2", "sZ",
        "sS", "sPT", "sAV", "sSM", "sPH", "sVH",
    ]
    sems = {}
    for n_ in sem_names:
        sems[n_] = nc.semaphore(n_).__enter__()
    S = sems

    def vsync(v, instr):
        instr.then_inc(S["sVC"], 1); B.inc("sVC")
        v.wait_ge(S["sVC"], B.get("sVC"))
    def ssync(sc, instr):
        instr.then_inc(S["sSC"], 1); B.inc("sSC")
        sc.wait_ge(S["sSC"], B.get("sSC"))

    gu_chunks = [(k * 1408, min(1408, I - k * 1408)) for k in range(8)]
    def gu_chunk_of(ib):
        return min(ib // 11, 7)

    with nc.Block() as blk:
        @blk.gpsimd
        def _(g):
            ci = [0]
            def cdma(o, i_):
                GP.guard(g); GP.attach(g.dma_start(out=o, in_=i_), ("c", ci[0])); ci[0] += 1
            cdma(c_idn_s[:, :], c_idn[:, :])
            for c in range(3):
                cdma(c_biv_s[:, c * 128:(c + 1) * 128], c_biv[c * 128:(c + 1) * 128, :])
            cdma(c_sblk_s[:, :], c_sblk[:, :])
            for c in range(16):
                cdma(c_ind_s[:, c * 16:(c + 1) * 16], c_ind[c * 128:(c + 1) * 128, :])
            cdma(m_vecs[0:ICn, 0:128], ag_in[:].rearrange("(a b) -> a b", b=128))
            cdma(m_vecs[0:ICn, 128:256], au_in[:].rearrange("(a b) -> a b", b=128))
            cdma(m_vecs[0:DC, 256:384], ad_in[:].rearrange("(a b) -> a b", b=128))
            cdma(m_vecs[0:DC, 384:512], b_q[:].rearrange("(a b) -> a b", b=128))
            cdma(m_vecs[0:DC, 512:640], b_k[:].rearrange("(a b) -> a b", b=128))
            cdma(m_vecs[0:DC, 640:768], b_v[:].rearrange("(a b) -> a b", b=128))
            cdma(m_bwB[0:1, 0:48], bw_in[:, :])
            GP.snap("consts")
            for k, (r0, nr) in enumerate(gu_chunks):
                GP.guard(g); GP.attach(g.dma_start(out=wg_b[r0:r0 + nr, :], in_=w_g[r0:r0 + nr, :]), ("cg", k))
                GP.guard(g); GP.attach(g.dma_start(out=wu_b[r0:r0 + nr, :], in_=w_u[r0:r0 + nr, :]), ("cu", k))
            for k, r0 in enumerate((0, 2048)):
                GP.guard(g); GP.attach(g.dma_start(out=wd_b[r0:r0 + 2048, :], in_=w_d[r0:r0 + 2048, :]), ("cd", k))
            for (wsrc, wdst, tg) in ((w_k, wk_b, "ck"), (w_v, wv_b, "cv"), (w_q, wq_b, "cq")):
                for k, r0 in enumerate((0, 2048)):
                    GP.guard(g); GP.attach(g.dma_start(out=wdst[r0:r0 + 2048, :], in_=wsrc[r0:r0 + 2048, :]), (tg, k))

        @blk.sync
        def _(s):
            def emit_tail(lt):
                s.wait_ge(S["sLNA"], lt + 1)
                SP.guard(s); SP.attach(s.dma_start(out=lnT_tr_out(lt), in_=m_lntm[:, :], transpose=True), ("lnt", lt))
                s.wait_ge(S["sEV0"], 8 * (lt + 1))
                SP.guard(s); SP.attach(s.dma_start(out=xres[:, :, lt * 128:(lt + 1) * 128].rearrange("c p j -> p c j"),
                                      in_=m_xstage[:, :].rearrange("p (c j) -> p c j", c=DC)), ("xw", lt))
            for lt in range(LT):
                if lt >= 2:
                    s.wait_ge(S["sPE0"], 8 * (lt - 1))
                    s.wait_ge(S["sLNA"], lt - 1)
                SP.guard(s); SP.attach(s.dma_start(out=ph0_x(lt % 2), in_=x_in[lt * 128:(lt + 1) * 128, :]), ("xl", lt))
                if lt >= 1:
                    emit_tail(lt - 1)
            emit_tail(LT - 1)
            SP.snap("ph0done")

        @blk.tensor
        def _(t):
            GP.wait(t, ("c", 0))
            for lt in range(LT):
                SP.wait(t, ("xl", lt))
                for grp in range(8):
                    ps = [PCp, PD][grp % 2]
                    if lt * 8 + grp >= 2:
                        t.wait_ge(S["sEV0"], lt * 8 + grp - 1)
                    for k in range(4):
                        dc = grp * 4 + k
                        last = t.transpose(ps[:, k * 128:(k + 1) * 128],
                                           ph0_x(lt % 2)[:, dc * 128:(dc + 1) * 128],
                                           c_idn_s[:, :])
                    last.then_inc(S["sPE0"], 1); B.inc("sPE0")

        @blk.scalar
        def _(sc):
            for lt in range(LT):
                SP.wait(sc, ("xl", lt))
                if lt >= 1:
                    sc.wait_ge(S["sVAR"], lt)
                sc.activation(ph0_sq, ph0_x(lt % 2), AF.Square).then_inc(S["sSQ"], 1); B.inc("sSQ")
                sc.wait_ge(S["sVAR"], lt + 1)
                ssync(sc, sc.activation(m_sc[:, 0:1], m_sc[:, 1:2], AF.Ln, bias=c_misc[:, 0:1]))
                sc.activation(m_rs[:, lt:lt + 1], m_sc[:, 0:1], AF.Exp, scale=c_misc[:, 1:2]).then_inc(S["sRST"], 1); B.inc("sRST")

        @blk.vector
        def _(v):
            v.memset(c_ones1[:, :], 1.0)
            v.memset(c_onesc[:, :], 1.0)
            v.memset(c_misc[:, 0:1], EPS)
            v.memset(c_misc[:, 1:2], -0.5)
            v.memset(c_misc[:, 2:3], 0.25)
            for lt in range(LT):
                SP.wait(v, ("xl", lt))
                xa = ph0_x(lt % 2)
                vsync(v, v.reduce_sum(m_xm[:, lt:lt + 1], xa, axis=AX.X))
                vsync(v, v.tensor_scalar(m_xm[:, lt:lt + 1], m_xm[:, lt:lt + 1], 1.0 / D, None, ALU.mult))
                v.wait_ge(S["sSQ"], lt + 1)
                vsync(v, v.reduce_sum(m_sc[:, 2:3], ph0_sq, axis=AX.X))
                vsync(v, v.tensor_mul(m_sc[:, 3:4], m_xm[:, lt:lt + 1], m_xm[:, lt:lt + 1]))
                v.scalar_tensor_tensor(m_sc[:, 1:2], m_sc[:, 2:3], 1.0 / D, m_sc[:, 3:4],
                                       ALU.mult, ALU.subtract).then_inc(S["sVAR"], 1); B.inc("sVAR")
                v.wait_ge(S["sRST"], lt + 1)
                if lt >= 1:
                    SP.wait(v, ("lnt", lt - 1))
                v.tensor_scalar(m_lntm[:, :], xa, m_xm[:, lt:lt + 1], m_rs[:, lt:lt + 1],
                                ALU.subtract, ALU.mult).then_inc(S["sLNA"], 1); B.inc("sLNA")
                for grp in range(8):
                    ps = [PCp, PD][grp % 2]
                    v.wait_ge(S["sPE0"], lt * 8 + grp + 1)
                    if lt >= 1 and grp == 0:
                        SP.wait(v, ("xw", lt - 1))
                    v.tensor_copy(m_xstage[:, grp * 512:(grp + 1) * 512],
                                  ps[:, 0:512]).then_inc(S["sEV0"], 1); B.inc("sEV0")

    with nc.Block() as blk:
        @blk.tensor
        def _(t):
            GP.wait(t, "consts")
            t.transpose(PA[:, 0:ICn], m_vecs[0:ICn, 0:128], c_idn_s[0:ICn, 0:ICn])
            t.transpose(PB[:, 0:ICn], m_vecs[0:ICn, 128:256], c_idn_s[0:ICn, 0:ICn])
            t.transpose(PCp[:, 0:DC], m_vecs[0:DC, 256:384], c_idn_s[0:DC, 0:DC])
            t.transpose(PCp[:, 128:128 + DC], m_vecs[0:DC, 384:512], c_idn_s[0:DC, 0:DC])
            t.transpose(PD[:, 0:DC], m_vecs[0:DC, 512:640], c_idn_s[0:DC, 0:DC])
            last = t.transpose(PD[:, 128:128 + DC], m_vecs[0:DC, 640:768], c_idn_s[0:DC, 0:DC])
            last.then_inc(S["sBM"], 1); B.inc("sBM")
            t.matmul(PD[:, 256:304], c_ones1[:, :], m_bwB[0:1, 0:48],
                     start=True, stop=True).then_inc(S["sBM"], 1); B.inc("sBM")

        @blk.vector
        def _(v):
            v.wait_ge(S["sBM"], 1)
            v.tensor_copy(m_agF[:, :], PA[:, 0:ICn])
            v.tensor_copy(m_auF[:, :], PB[:, 0:ICn])
            v.tensor_scalar(m_adh[:, :], PCp[:, 0:DC], 0.5, None, ALU.mult)
            v.tensor_copy(m_bF[:, 0:DC], PCp[:, 128:128 + DC])
            v.tensor_copy(m_bF[:, DC:2 * DC], PD[:, 0:DC])
            v.tensor_copy(m_bF[:, 2 * DC:3 * DC], PD[:, 128:128 + DC])
            v.wait_ge(S["sBM"], 2)
            vsync(v, v.tensor_copy(m_bwB[:, :], PD[:, 256:304]))
            for h in range(NH):
                vsync(v, v.scalar_tensor_tensor(bacb(0), c_biv_s[:, 0:128], m_bwB[:, 3 * h:3 * h + 1],
                                                c_sblk_s[:, :], ALU.mult, ALU.add))
                vsync(v, v.scalar_tensor_tensor(bacb(1), c_biv_s[:, 128:256], m_bwB[:, 3 * h + 1:3 * h + 2],
                                                bacb(0), ALU.mult, ALU.add))
                vsync(v, v.scalar_tensor_tensor(m_bmx[:, h * 128:(h + 1) * 128], c_biv_s[:, 256:384],
                                                m_bwB[:, 3 * h + 2:3 * h + 3], bacb(1), ALU.mult, ALU.add))

    def ffn_gu_block(p):
        bGU = B.get("sGU"); bSI = B.get("sSI"); bH = B.get("sH")
        with nc.Block() as blk:
            @blk.sync
            def _(s):
                for ib in range(ICn):
                    if ib >= 2:
                        s.wait_ge(S["sGU"], bGU + 2 * (ib - 1))
                    if p == 0:
                        GP.wait(s, ("cg", gu_chunk_of(ib)))
                    SP.guard(s); SP.attach(s.dma_start(out=gusl_v(0, ib % 2, 0, 4096).rearrange("p (c j) -> p c j", c=DC),
                                          in_=wg_b[ib * 128:(ib + 1) * 128, :], transpose=True), (p, "gs", ib))
                    if p == 0:
                        GP.wait(s, ("cu", gu_chunk_of(ib)))
                    SP.guard(s); SP.attach(s.dma_start(out=gusl_v(1, ib % 2, 0, 4096).rearrange("p (c j) -> p c j", c=DC),
                                          in_=wu_b[ib * 128:(ib + 1) * 128, :], transpose=True), (p, "us", ib))
                    if ib >= 1:
                        s.wait_ge(S["sH"], bH + ib)
                        for th in range(2):
                            SP.guard(s); SP.attach(s.dma_start(out=hT_d[th, ib - 1, :, :],
                                                  in_=hbuf((ib - 1) % 2)[:, th * 512:(th + 1) * 512]),
                                      (p, "hw", ib - 1, th))
                s.wait_ge(S["sH"], bH + ICn)
                for th in range(2):
                    SP.guard(s); SP.attach(s.dma_start(out=hT_d[th, ICn - 1, :, :],
                                          in_=hbuf((ICn - 1) % 2)[:, th * 512:(th + 1) * 512]),
                              (p, "hw", ICn - 1, th))

            @blk.tensor
            def _(t):
                if p == 0:
                    SP.wait(t, "ph0done")
                for ib in range(ICn):
                    SP.wait(t, (p, "gs", ib))
                    SP.wait(t, (p, "us", ib))
                    if ib >= 1:
                        t.wait_ge(S["sSI"], bSI + ib)
                    for dc in range(DC):
                        for th in range(2):
                            last = t.matmul(PA[:, th * 512:(th + 1) * 512],
                                            gusl_v(0, ib % 2, dc * 128, 128),
                                            lnT_v(dc, th * 512, 512),
                                            start=(dc == 0), stop=(dc == DC - 1))
                    last.then_inc(S["sGU"], 1); B.inc("sGU")
                    if ib >= 1:
                        t.wait_ge(S["sH"], bH + ib)
                    for dc in range(DC):
                        for th in range(2):
                            last = t.matmul(PB[:, th * 512:(th + 1) * 512],
                                            gusl_v(1, ib % 2, dc * 128, 128),
                                            lnT_v(dc, th * 512, 512),
                                            start=(dc == 0), stop=(dc == DC - 1))
                    last.then_inc(S["sGU"], 1); B.inc("sGU")

            @blk.scalar
            def _(sc):
                for ib in range(ICn):
                    sc.wait_ge(S["sGU"], bGU + 2 * ib + 1)
                    if ib >= 1:
                        sc.wait_ge(S["sH"], bH + ib)
                    sc.activation(m_tsg[:, :], PA[:, :], AF.Silu,
                                  scale=m_agF[:, ib:ib + 1]).then_inc(S["sSI"], 1); B.inc("sSI")

            @blk.vector
            def _(v):
                for ib in range(ICn):
                    v.wait_ge(S["sGU"], bGU + 2 * ib + 2)
                    v.wait_ge(S["sSI"], bSI + ib + 1)
                    if ib >= 2:
                        SP.wait(v, (p, "hw", ib - 2, 0))
                        SP.wait(v, (p, "hw", ib - 2, 1))
                    elif p == 1:
                        SP.wait(v, (0, "hw", 84 + ib, 0))
                        SP.wait(v, (0, "hw", 84 + ib, 1))
                    v.scalar_tensor_tensor(hbuf(ib % 2), PB[:, :], m_auF[:, ib:ib + 1],
                                           m_tsg[:, :], ALU.mult, ALU.mult).then_inc(S["sH"], 1); B.inc("sH")

    def ffn_down_block(p):
        bDN = B.get("sDN"); bXN = B.get("sXN")
        with nc.Block() as blk:
            @blk.sync
            def _(s):
                for th in range(2):
                    if th == 1:
                        s.wait_ge(S["sDN"], bDN + DC)
                    for piece in range(2):
                        SP.guard(s); SP.attach(s.dma_start(
                            out=r_big[:, piece * 22016:piece * 22016 + 22016].rearrange("p (a j) -> p a j", a=43),
                            in_=hT_d[th, piece * 43:(piece + 1) * 43, :, :].rearrange("a p j -> p a j")),
                            (p, "hl", th, piece))
                    for dcn in range(DC):
                        it = th * DC + dcn
                        if it >= 2:
                            s.wait_ge(S["sDN"], bDN + it - 1)
                        if p == 0 and th == 0:
                            GP.wait(s, ("cd", dcn // 16))
                        SP.guard(s); SP.attach(s.dma_start(out=wdts_v(it % 2, 0, 11008).rearrange("p (a j) -> p a j", a=ICn),
                                              in_=wd_b[dcn * 128:(dcn + 1) * 128, :], transpose=True),
                                  (p, "wd", th, dcn))
                        if it >= 2:
                            s.wait_ge(S["sXN"], bXN + it - 1)
                        SP.guard(s); SP.attach(s.dma_start(out=xslb(it % 2)[:, 0:512],
                                              in_=xres[dcn, :, th * 512:(th + 1) * 512]), (p, "dx", th, dcn))
                        if it >= 1:
                            s.wait_ge(S["sXN"], bXN + it)
                            pdcn = (it - 1) % DC; pth = (it - 1) // DC
                            SP.guard(s); SP.attach(s.dma_start(out=xres[pdcn, :, pth * 512:(pth + 1) * 512],
                                                  in_=xnb((it - 1) % 2)[:, 0:512]), (p, "db", it - 1))
                it = 2 * DC
                s.wait_ge(S["sXN"], bXN + it)
                SP.guard(s); SP.attach(s.dma_start(out=xres[DC - 1, :, 512:1024],
                                      in_=xnb((it - 1) % 2)[:, 0:512]), (p, "db", it - 1))

            @blk.tensor
            def _(t):
                for th in range(2):
                    SP.wait(t, (p, "hl", th, 0))
                    SP.wait(t, (p, "hl", th, 1))
                    for dcn in range(DC):
                        it = th * DC + dcn
                        SP.wait(t, (p, "wd", th, dcn))
                        ps = [PCp, PD][it % 2]
                        if it >= 2:
                            t.wait_ge(S["sXN"], bXN + it - 1)
                        for ic in range(ICn):
                            last = t.matmul(ps[:, 0:512], wdts_v(it % 2, ic * 128, 128),
                                            hT_v(ic, 0, 512), start=(ic == 0), stop=(ic == ICn - 1))
                        last.then_inc(S["sDN"], 1); B.inc("sDN")

            @blk.vector
            def _(v):
                for th in range(2):
                    for dcn in range(DC):
                        it = th * DC + dcn
                        v.wait_ge(S["sDN"], bDN + it + 1)
                        SP.wait(v, (p, "dx", th, dcn))
                        ps = [PCp, PD][it % 2]
                        if it >= 2:
                            SP.wait(v, (p, "db", it - 2))
                        elif p == 1:
                            SP.wait(v, ("ab", NH - 1, it))
                        v.scalar_tensor_tensor(xnb(it % 2)[:, 0:512], ps[:, 0:512],
                                               m_adh[:, dcn:dcn + 1], xslb(it % 2)[:, 0:512],
                                               ALU.mult, ALU.add).then_inc(S["sXN"], 1); B.inc("sXN")

    def ln_block(li):
        bXB = B.get("sXB"); bQQ = B.get("sQQ"); bST = B.get("sST")
        bLB = B.get("sLB"); bLNA = B.get("sLNA")
        with nc.Block() as blk:
            @blk.sync
            def _(s):
                SP.snap(("lnpre", li))
                for dc in range(DC):
                    if dc >= 2:
                        s.wait_ge(S["sXB"], bXB + dc - 1)
                    SP.guard(s); SP.attach(s.dma_start(out=xslb(dc % 2), in_=xres[dc, :, :]), (li, "ls", dc))
                s.wait_ge(S["sXB"], bXB + DC)
                for dc in range(DC):
                    if dc >= 2:
                        s.wait_ge(S["sLNA"], bLNA + dc - 1)
                    SP.guard(s); SP.attach(s.dma_start(out=xslb(dc % 2), in_=xres[dc, :, :]), (li, "la", dc))

            @blk.vector
            def _(v):
                for dc in range(DC):
                    SP.wait(v, (li, "ls", dc))
                    if dc >= 2:
                        v.wait_ge(S["sST"], bST + dc - 1)
                    v.tensor_copy(xbb(dc % 2), xslb(dc % 2)).then_inc(S["sXB"], 1); B.inc("sXB")
                v.wait_ge(S["sST"], bST + DC)
                vsync(v, v.tensor_copy(m_srow[0:1, :], PA[0:1, 0:1024]))
                vsync(v, v.tensor_copy(m_srow2[0:1, :], PB[0:1, 0:1024]))
                vsync(v, v.tensor_scalar(m_srow[0:1, :], m_srow[0:1, :], 1.0 / D, None, ALU.mult))
                vsync(v, v.tensor_mul(m_rrow[0:1, :], m_srow[0:1, :], m_srow[0:1, :]))
                v.scalar_tensor_tensor(m_srow2[0:1, :], m_srow2[0:1, :], 1.0 / D,
                                       m_rrow[0:1, :], ALU.mult, ALU.subtract).then_inc(S["sLB"], 1); B.inc("sLB")
                v.wait_ge(S["sLB"], bLB + 2)
                vsync(v, v.tensor_mul(m_rrow[0:1, :], m_srow[0:1, :], m_srow2[0:1, :]))
                v.tensor_scalar(m_rrow[0:1, :], m_rrow[0:1, :], -1.0, None, ALU.mult).then_inc(S["sLB"], 1); B.inc("sLB")
                v.wait_ge(S["sST"], bST + DC + 1)
                vsync(v, v.tensor_copy(m_ab0[:, :], PCp[:, :]))
                vsync(v, v.tensor_copy(m_ab1[:, :], PD[:, :]))
                v.engine_nop().then_inc(S["sLB"], 1); B.inc("sLB")
                SP.wait(v, ("lnpre", li))
                for dc in range(DC):
                    SP.wait(v, (li, "la", dc))
                    vsync(v, v.tensor_mul(xnb(0), xslb(dc % 2), m_ab0[:, :]))
                    v.tensor_add(lnT_v(dc, 0, 1024), xnb(0), m_ab1[:, :]).then_inc(S["sLNA"], 1); B.inc("sLNA")

            @blk.scalar
            def _(sc):
                for dc in range(DC):
                    sc.wait_ge(S["sXB"], bXB + dc + 1)
                    if dc >= 2:
                        sc.wait_ge(S["sST"], bST + dc - 1)
                    sc.activation(sqb(dc % 2), xbb(dc % 2), AF.Square).then_inc(S["sQQ"], 1); B.inc("sQQ")
                sc.wait_ge(S["sLB"], bLB + 1)
                ssync(sc, sc.activation(m_srow2[0:1, :], m_srow2[0:1, :], AF.Ln, bias=c_misc[0:1, 0:1]))
                sc.activation(m_srow2[0:1, :], m_srow2[0:1, :], AF.Exp, scale=c_misc[0:1, 1:2]).then_inc(S["sLB"], 1); B.inc("sLB")

            @blk.tensor
            def _(t):
                for dc in range(DC):
                    t.wait_ge(S["sXB"], bXB + dc + 1)
                    t.wait_ge(S["sQQ"], bQQ + dc + 1)
                    for half in range(2):
                        t.matmul(PA[0:1, half * 512:(half + 1) * 512], c_onesc[:, :],
                                 xbb(dc % 2)[:, half * 512:(half + 1) * 512],
                                 start=(dc == 0), stop=(dc == DC - 1), skip_group_check=True)
                        last = t.matmul(PB[0:1, half * 512:(half + 1) * 512], c_onesc[:, :],
                                        sqb(dc % 2)[:, half * 512:(half + 1) * 512],
                                        start=(dc == 0), stop=(dc == DC - 1), skip_group_check=True)
                    last.then_inc(S["sST"], 1); B.inc("sST")
                t.wait_ge(S["sLB"], bLB + 3)
                for half in range(2):
                    t.matmul(PCp[:, half * 512:(half + 1) * 512], c_ones1[:, :],
                             m_srow2[0:1, half * 512:(half + 1) * 512], start=True, stop=True)
                    last = t.matmul(PD[:, half * 512:(half + 1) * 512], c_ones1[:, :],
                                    m_rrow[0:1, half * 512:(half + 1) * 512], start=True, stop=True)
                last.then_inc(S["sST"], 1); B.inc("sST")

    def proj_block():
        bPJ = B.get("sPJ"); bEVJ = B.get("sEVJ"); bQQ = B.get("sQQ")
        bMX = B.get("sMX"); bZ = B.get("sZ")
        def evj_k(dmc): return bEVJ + dmc + 1
        def evj_v(dmc): return bEVJ + 34 + dmc
        def evj_qt(dmc): return bEVJ + 65 + 2 * dmc + 1
        def evj_qe(dmc): return bEVJ + 65 + 2 * dmc + 2
        def mx_kst(dmc): return bMX + dmc + 1
        def mx_qmx(dmc): return bMX + 32 + 2 * dmc + 1
        def mx_qst(dmc): return bMX + 32 + 2 * dmc + 2
        specs = [("k", wk_b, "ck"), ("v", wv_b, "cv"), ("q", wq_b, "cq")]
        with nc.Block() as blk:
            @blk.sync
            def _(s):
                for wi, (kind, wbuf, ctg) in enumerate(specs):
                    for dmc in range(DC):
                        it = wi * DC + dmc
                        if it >= 2:
                            s.wait_ge(S["sPJ"], bPJ + it - 1)
                        GP.wait(s, (ctg, dmc // 16))
                        SP.guard(s); SP.attach(s.dma_start(out=pjsl_v(it % 2, 0, 4096).rearrange("p (c j) -> p c j", c=DC),
                                              in_=wbuf[dmc * 128:(dmc + 1) * 128, :], transpose=True), ("ps", it))
                        if kind == "k" and dmc >= 1:
                            s.wait_ge(S["sEVJ"], evj_k(dmc - 1))
                            SP.guard(s); SP.attach(s.dma_start(out=kin_k[(dmc - 1) * 128:dmc * 128, :],
                                                  in_=qtb((dmc - 1) % 2)), ("kw", dmc - 1))
                        if kind == "v" and dmc >= 1:
                            s.wait_ge(S["sEVJ"], evj_v(dmc - 1))
                            SP.guard(s); SP.attach(s.dma_start(out=kin_v[(dmc - 1) * 128:dmc * 128, :],
                                                  in_=qtb((DC + dmc - 1) % 2)), ("vw", dmc - 1))
                    if kind == "k":
                        s.wait_ge(S["sEVJ"], evj_k(DC - 1))
                        SP.guard(s); SP.attach(s.dma_start(out=kin_k[(DC - 1) * 128:DC * 128, :],
                                              in_=qtb((DC - 1) % 2)), ("kw", DC - 1))
                        SP.snap("kdone")
                        s.wait_ge(S["sEVJ"], bEVJ + 33)
                        SP.guard(s); SP.attach(s.dma_start(out=k2_in[:, :], in_=m_k2[:, :]), ("k2w",))
                        SP.snap("k2done")
                    if kind == "v":
                        s.wait_ge(S["sEVJ"], evj_v(DC - 1))
                        SP.guard(s); SP.attach(s.dma_start(out=kin_v[(DC - 1) * 128:DC * 128, :],
                                              in_=qtb((2 * DC - 1) % 2)), ("vw", DC - 1))
                        SP.snap("vdone")
                s.wait_ge(S["sCC2"], 1)
                SP.guard(s); SP.attach(s.dma_start(out=m_k2[:, :], in_=k2_g[:, :]), ("zl",))

            @blk.gpsimd
            def _(g):
                SP.wait(g, "kdone")
                g.collective_compute("AllGather", ALU.bypass, replica_groups=[list(range(cc))],
                                     ins=[kin_k.ap().opt()], outs=[kg.ap().opt()]).then_inc(S["sCCk"]); B.inc("sCCk")
                SP.wait(g, "k2done")
                g.collective_compute("AllReduce", ALU.max, replica_groups=[list(range(cc))],
                                     ins=[k2_in.ap().opt()], outs=[k2_g.ap().opt()]).then_inc(S["sCC2"]); B.inc("sCC2")
                SP.wait(g, "vdone")
                g.collective_compute("AllGather", ALU.bypass, replica_groups=[list(range(cc))],
                                     ins=[kin_v.ap().opt()], outs=[vg.ap().opt()]).then_inc(S["sCCv"]); B.inc("sCCv")

            @blk.tensor
            def _(t):
                for wi, (kind, wbuf, ctg) in enumerate(specs):
                    for dmc in range(DC):
                        it = wi * DC + dmc
                        SP.wait(t, ("ps", it))
                        ps = [PA, PB][it % 2]
                        if it >= 2:
                            pit = it - 2
                            pw = pit // DC; pd = pit % DC
                            pv = [evj_k, evj_v, evj_qt][pw](pd)
                            t.wait_ge(S["sEVJ"], pv)
                        for dc in range(DC):
                            for th in range(2):
                                last = t.matmul(ps[:, th * 512:(th + 1) * 512],
                                                pjsl_v(it % 2, dc * 128, 128),
                                                lnT_v(dc, th * 512, 512),
                                                start=(dc == 0), stop=(dc == DC - 1))
                        last.then_inc(S["sPJ"], 1); B.inc("sPJ")
                        if kind == "q":
                            h = dmc // 2
                            t.wait_ge(S["sEVJ"], evj_qt(dmc))
                            for th in range(2):
                                last = t.matmul(PCp[:, th * 512:(th + 1) * 512],
                                                m_bmx[:, h * 128:(h + 1) * 128],
                                                qtb(it % 2)[:, th * 512:(th + 1) * 512],
                                                start=True, stop=True, skip_group_check=True)
                            last.then_inc(S["sMX"], 1); B.inc("sMX")
                        if kind in ("k", "q"):
                            qoff = 0 if kind == "k" else DC
                            t.wait_ge(S["sQQ"], bQQ + qoff + dmc + 1)
                            for half in range(2):
                                last = t.matmul(PD[0:16, half * 512:(half + 1) * 512],
                                                c_ind_s[:, (dmc // 2) * 16:(dmc // 2) * 16 + 16],
                                                sqb(dmc % 2)[:, half * 512:(half + 1) * 512],
                                                start=(dmc == 0), stop=(dmc == DC - 1), skip_group_check=True)
                            last.then_inc(S["sMX"], 1); B.inc("sMX")
                t.wait_ge(S["sZ"], bZ + 2)
                t.transpose(PCp[0:1, 0:16], m_zs[0:16, 3:4], c_idn_s[0:16, 0:16]).then_inc(S["sZ"], 1); B.inc("sZ")
                t.wait_ge(S["sZ"], bZ + 4)
                t.matmul(PD[:, 0:16], c_ones1[:, :], m_srow[0:1, 0:16],
                         start=True, stop=True).then_inc(S["sZ"], 1); B.inc("sZ")

            @blk.vector
            def _(v):
                for wi, (kind, wbuf, ctg) in enumerate(specs):
                    for dmc in range(DC):
                        it = wi * DC + dmc
                        v.wait_ge(S["sPJ"], bPJ + it + 1)
                        ps = [PA, PB][it % 2]
                        if kind == "k":
                            if dmc >= 2:
                                SP.wait(v, ("kw", dmc - 2))
                            v.tensor_scalar(qtb(it % 2), ps[:, :], m_bF[:, DC + dmc:DC + dmc + 1],
                                            None, ALU.add).then_inc(S["sEVJ"], 1); B.inc("sEVJ")
                        elif kind == "v":
                            if dmc >= 2:
                                SP.wait(v, ("vw", dmc - 2))
                            else:
                                SP.wait(v, ("kw", DC - 2 + dmc))
                            v.tensor_scalar(qtb(it % 2), ps[:, :], m_bF[:, 2 * DC + dmc:2 * DC + dmc + 1],
                                            None, ALU.add).then_inc(S["sEVJ"], 1); B.inc("sEVJ")
                        else:
                            if dmc >= 2:
                                v.wait_ge(S["sMX"], mx_qmx(dmc - 2))
                            else:
                                SP.wait(v, ("vw", DC - 2 + dmc))
                            v.tensor_scalar(qtb(it % 2), ps[:, :], m_bF[:, dmc:dmc + 1],
                                            None, ALU.add).then_inc(S["sEVJ"], 1); B.inc("sEVJ")
                            v.wait_ge(S["sMX"], mx_qmx(dmc))
                            v.tensor_copy(qeff_v(dmc, 0, 1024), PCp[:, :]).then_inc(S["sEVJ"], 1); B.inc("sEVJ")
                    if kind == "k":
                        v.wait_ge(S["sMX"], mx_kst(DC - 1))
                        v.tensor_copy(m_k2[:, :], PD[0:16, 0:1024]).then_inc(S["sEVJ"], 1); B.inc("sEVJ")
                v.wait_ge(S["sMX"], mx_qst(DC - 1))
                vsync(v, v.tensor_copy(m_q2[:, :], PD[0:16, 0:1024]))
                vsync(v, v.reduce_max(m_zs[:, 1:2], m_q2[:, :], axis=AX.X))
                SP.wait(v, ("zl",))
                vsync(v, v.reduce_max(m_zs[:, 0:1], m_k2[:, :], axis=AX.X))
                v.tensor_mul(m_zs[:, 2:3], m_zs[:, 0:1], m_zs[:, 1:2]).then_inc(S["sZ"], 1); B.inc("sZ")
                v.wait_ge(S["sZ"], bZ + 3)
                v.tensor_copy(m_srow[0:1, 0:16], PCp[0:1, 0:16]).then_inc(S["sZ"], 1); B.inc("sZ")
                v.wait_ge(S["sZ"], bZ + 5)
                v.tensor_scalar(m_negz[:, :], PD[:, 0:16], -1.0, None, ALU.mult).then_inc(S["sZ"], 1); B.inc("sZ")

            @blk.scalar
            def _(sc):
                for dmc in range(DC):
                    sc.wait_ge(S["sEVJ"], evj_k(dmc))
                    if dmc >= 2:
                        sc.wait_ge(S["sMX"], mx_kst(dmc - 2))
                    sc.activation(sqb(dmc % 2), qtb(dmc % 2), AF.Square).then_inc(S["sQQ"], 1); B.inc("sQQ")
                for dmc in range(DC):
                    sc.wait_ge(S["sEVJ"], evj_qe(dmc))
                    if dmc >= 2:
                        sc.wait_ge(S["sMX"], mx_qst(dmc - 2))
                    sc.activation(sqb(dmc % 2), qeff_v(dmc, 0, 1024), AF.Square).then_inc(S["sQQ"], 1); B.inc("sQQ")
                sc.wait_ge(S["sZ"], bZ + 1)
                sc.activation(m_zs[:, 3:4], m_zs[:, 2:3], AF.Sqrt, scale=c_misc[0:NH, 2:3]).then_inc(S["sZ"], 1); B.inc("sZ")

    def attn_block():
        NMC = cc * 8
        bS = B.get("sS"); bPT = B.get("sPT"); bAV = B.get("sAV"); bSM = B.get("sSM")
        bPH = B.get("sPH"); bVH = B.get("sVH")
        def sm_done(gg):
            h_, m_ = gg // NMC, gg % NMC
            return bSM + (NMC + 1) * h_ + m_ + 1
        def vh_epi(h_, dc2):
            return bVH + 5 * h_ + 4 + dc2
        with nc.Block() as blk:
            @blk.sync
            def _(s):
                s.wait_ge(S["sCCk"], 1)
                for r in range(cc):
                    for cd in range(2):
                        SP.guard(s); SP.attach(s.dma_start(out=kh_v(0, cd, r * 1024, 1024),
                                              in_=kg[r * D + cd * 128: r * D + (cd + 1) * 128, :]), None)
                SP.snap(("khs", 0))
                s.wait_ge(S["sCCv"], 1)
                for h in range(NH):
                    def vchunk(mc):
                        gg = h * NMC + mc
                        if gg >= 4:
                            s.wait_ge(S["sAV"], bAV + gg - 3)
                        r, lm = mc // 8, mc % 8
                        SP.guard(s); SP.attach(s.dma_start(out=vtb(mc % 4),
                                              in_=vg[r * D + h * 256: r * D + h * 256 + 256,
                                                     lm * 128:(lm + 1) * 128],
                                              transpose=True), ("vt", h, mc))
                    for mc in range(min(16, NMC)):
                        vchunk(mc)
                    if h + 1 < NH:
                        if h >= 1:
                            s.wait_ge(S["sS"], bS + NMC * h)
                        for r in range(cc):
                            for cd in range(2):
                                SP.guard(s); SP.attach(s.dma_start(out=kh_v((h + 1) % 2, cd, r * 1024, 1024),
                                                      in_=kg[r * D + (h + 1) * 256 + cd * 128:
                                                             r * D + (h + 1) * 256 + (cd + 1) * 128, :]), None)
                        SP.snap(("khs", h + 1))
                    for mc in range(16, NMC) if NMC > 16 else []:
                        vchunk(mc)
                    for dc2 in range(2):
                        if h >= 1:
                            s.wait_ge(S["sVH"], vh_epi(h - 1, dc2))
                        SP.guard(s); SP.attach(s.dma_start(out=xslb(dc2), in_=xres[2 * h + dc2, :, :]), ("cx", h, dc2))
                    if h >= 1:
                        for dc2 in range(2):
                            s.wait_ge(S["sVH"], vh_epi(h - 1, dc2))
                            SP.guard(s); SP.attach(s.dma_start(out=xres[2 * (h - 1) + dc2, :, :],
                                                  in_=xnb(dc2)), ("ab", h - 1, dc2))
                for dc2 in range(2):
                    s.wait_ge(S["sVH"], vh_epi(NH - 1, dc2))
                    SP.guard(s); SP.attach(s.dma_start(out=xres[2 * (NH - 1) + dc2, :, :],
                                          in_=xnb(dc2)), ("ab", NH - 1, dc2))

            @blk.tensor
            def _(t):
                for h in range(NH):
                    SP.wait(t, ("khs", h))
                    if h >= 1:
                        t.wait_ge(S["sVH"], bVH + 5 * (h - 1) + 3)
                    for mc in range(NMC):
                        gg = h * NMC + mc
                        par = mc % 2
                        ps = [PA, PCp][par]
                        if gg >= 2:
                            t.wait_ge(S["sPT"], bPT + gg - 1)
                        for cd in range(2):
                            for lt2 in range(2):
                                last = t.matmul(ps[:, lt2 * 512:(lt2 + 1) * 512],
                                                kh_v(h % 2, cd, mc * 128, 128),
                                                qeff_v(2 * h + cd, lt2 * 512, 512),
                                                start=(cd == 0), stop=(cd == 1), skip_group_check=True)
                        last.then_inc(S["sS"], 1); B.inc("sS")
                        if mc >= 1:
                            if mc == 1 and h >= 1:
                                t.wait_ge(S["sVH"], vh_epi(h - 1, 1))
                            t.wait_ge(S["sPT"], bPT + gg)
                            SP.wait(t, ("vt", h, mc - 1))
                            for dc2 in range(2):
                                po = [PB, PD][dc2]
                                for lt2 in range(2):
                                    last = t.matmul(po[:, lt2 * 512:(lt2 + 1) * 512],
                                                    vtb((mc - 1) % 4)[:, dc2 * 128:(dc2 + 1) * 128],
                                                    ptb((mc - 1) % 2)[:, lt2 * 512:(lt2 + 1) * 512],
                                                    start=(mc - 1 == 0), stop=False, skip_group_check=True)
                            last.then_inc(S["sAV"], 1); B.inc("sAV")
                    t.wait_ge(S["sPT"], bPT + h * NMC + NMC)
                    SP.wait(t, ("vt", h, NMC - 1))
                    for dc2 in range(2):
                        po = [PB, PD][dc2]
                        for lt2 in range(2):
                            last = t.matmul(po[:, lt2 * 512:(lt2 + 1) * 512],
                                            vtb((NMC - 1) % 4)[:, dc2 * 128:(dc2 + 1) * 128],
                                            ptb((NMC - 1) % 2)[:, lt2 * 512:(lt2 + 1) * 512],
                                            start=False, stop=True, skip_group_check=True)
                    last.then_inc(S["sAV"], 1); B.inc("sAV")
                    t.wait_ge(S["sSM"], bSM + (NMC + 1) * h + NMC + 1)
                    for k in range(8):
                        last = t.transpose(PA[:, k * 32:(k + 1) * 32], m_tree[0:32, k * 128:(k + 1) * 128],
                                           c_idn_s[0:32, 0:32])
                    last.then_inc(S["sPH"], 1); B.inc("sPH")
                    t.wait_ge(S["sVH"], bVH + 5 * h + 1)
                    for k in range(8):
                        last = t.transpose(PCp[0:1, k * 128:(k + 1) * 128], m_rinv[:, k:k + 1], c_idn_s[:, :])
                    last.then_inc(S["sPH"], 1); B.inc("sPH")
                    t.wait_ge(S["sVH"], bVH + 5 * h + 2)
                    for lt2 in range(2):
                        last = t.matmul(PA[:, lt2 * 512:(lt2 + 1) * 512], c_ones1[:, :],
                                        m_rrow[0:1, lt2 * 512:(lt2 + 1) * 512], start=True, stop=True)
                    last.then_inc(S["sPH"], 1); B.inc("sPH")

            @blk.scalar
            def _(sc):
                for h in range(NH):
                    for mc in range(NMC):
                        gg = h * NMC + mc
                        par = mc % 2
                        ps = [PA, PCp][par]
                        sc.wait_ge(S["sS"], bS + gg + 1)
                        if gg >= 2:
                            sc.wait_ge(S["sAV"], bAV + gg - 1)
                            sc.wait_ge(S["sSM"], sm_done(gg - 2))
                        sc.activation(ptb(par), ps[:, :], AF.Exp,
                                      bias=m_negz[:, h:h + 1]).then_inc(S["sPT"], 1); B.inc("sPT")

            @blk.vector
            def _(v):
                for h in range(NH):
                    for mc in range(NMC):
                        v.wait_ge(S["sPT"], bPT + h * NMC + mc + 1)
                        if mc >= 1:
                            v.wait_ge(S["sSM"], B.get("sSM"))
                        if mc == 0:
                            ins = v.tensor_copy(m_sum[:, :], ptb(0))
                        else:
                            ins = v.tensor_add(m_sum[:, :], m_sum[:, :], ptb(mc % 2))
                        ins.then_inc(S["sSM"], 1); B.inc("sSM")
                    v.wait_ge(S["sSM"], B.get("sSM"))
                    vsync(v, v.tensor_copy(m_tree[0:64, :], m_sum[64:128, :]))
                    vsync(v, v.tensor_add(m_tree[0:64, :], m_tree[0:64, :], m_sum[0:64, :]))
                    vsync(v, v.tensor_copy(m_rb[0:32, :], m_tree[32:64, :]))
                    v.tensor_add(m_tree[0:32, :], m_tree[0:32, :], m_rb[0:32, :]).then_inc(S["sSM"], 1); B.inc("sSM")
                    v.wait_ge(S["sPH"], bPH + 3 * h + 1)
                    for k in range(8):
                        ins = v.reduce_sum(m_rinv[:, k:k + 1], PA[:, k * 32:(k + 1) * 32], axis=AX.X)
                    vsync(v, ins)
                    v.reciprocal(m_rinv[:, :], m_rinv[:, :]).then_inc(S["sVH"], 1); B.inc("sVH")
                    v.wait_ge(S["sPH"], bPH + 3 * h + 2)
                    v.tensor_copy(m_rrow[0:1, :], PCp[0:1, 0:1024]).then_inc(S["sVH"], 1); B.inc("sVH")
                    v.wait_ge(S["sPH"], bPH + 3 * h + 3)
                    vsync(v, v.tensor_copy(m_rb[:, :], PA[:, :]))
                    v.engine_nop().then_inc(S["sVH"], 1); B.inc("sVH")
                    v.wait_ge(S["sAV"], bAV + NMC * (h + 1))
                    for dc2 in range(2):
                        po = [PB, PD][dc2]
                        SP.wait(v, ("cx", h, dc2))
                        if h >= 1:
                            SP.wait(v, ("ab", h - 1, dc2))
                        else:
                            SP.wait(v, (0, "db", 62 + dc2))
                        vsync(v, v.tensor_mul(m_xup[:, :], po[:, :], m_rb[:, :]))
                        v.tensor_add(xnb(dc2), m_xup[:, :], xslb(dc2)).then_inc(S["sVH"], 1); B.inc("sVH")

    def out_block():
        bPE0 = B.get("sPE0"); bEV0 = B.get("sEV0")
        with nc.Block() as blk:
            @blk.sync
            def _(s):
                for lt in range(LT):
                    for gb in range(4):
                        if lt * 4 + gb >= 2:
                            s.wait_ge(S["sPE0"], bPE0 + 2 * (lt * 4 + gb) - 2)
                        SP.guard(s); SP.attach(s.dma_start(out=xslb(gb % 2).rearrange("p (a j) -> p a j", a=8),
                                              in_=xres[gb * 8:(gb + 1) * 8, :, lt * 128:(lt + 1) * 128].rearrange("a p j -> p a j")),
                                  ("el", lt, gb))
                    s.wait_ge(S["sEV0"], bEV0 + 8 * (lt + 1))
                    SP.guard(s); SP.attach(s.dma_start(out=out[lt * 128:(lt + 1) * 128, :], in_=m_xstage[:, :]), ("out", lt))
                SP.snap("fin")
                SP.wait(s, "fin")

            @blk.tensor
            def _(t):
                for lt in range(LT):
                    for gb in range(4):
                        SP.wait(t, ("el", lt, gb))
                        for half in range(2):
                            ps = [PCp, PD][half]
                            idx = lt * 8 + gb * 2 + half
                            if idx >= 2:
                                t.wait_ge(S["sEV0"], bEV0 + idx - 1)
                            for k in range(4):
                                last = t.transpose(ps[:, k * 128:(k + 1) * 128],
                                                   xslb(gb % 2)[:, (half * 4 + k) * 128:(half * 4 + k + 1) * 128],
                                                   c_idn_s[:, :])
                            last.then_inc(S["sPE0"], 1); B.inc("sPE0")

            @blk.vector
            def _(v):
                for lt in range(LT):
                    for gb in range(4):
                        for half in range(2):
                            ps = [PCp, PD][half]
                            idx = lt * 8 + gb * 2 + half
                            v.wait_ge(S["sPE0"], bPE0 + idx + 1)
                            if lt >= 1 and gb == 0 and half == 0:
                                SP.wait(v, ("out", lt - 1))
                            v.tensor_copy(m_xstage[:, (gb * 2 + half) * 512:(gb * 2 + half + 1) * 512],
                                          ps[:, 0:512]).then_inc(S["sEV0"], 1); B.inc("sEV0")

            @blk.gpsimd
            def _(g):
                SP.wait(g, "fin")

    def raw_block():
        with nc.Block() as blk:
            @blk.sync
            def _(s):
                for dc in range(DC):
                    SP.guard(s); SP.attach(s.dma_start(out=dbgx[dc * 128:(dc + 1) * 128, :],
                                                       in_=xres[dc, :, :]), ("rw", dc))
                SP.guard(s); SP.attach(s.dma_start(out=dbgz[0:128, 0:NH], in_=m_negz[:, :]), ("rwz",))
                SP.guard(s); SP.attach(s.dma_start(out=dbgz[0:NH, NH:NH + 8], in_=m_zs[:, :]), ("rwz2",))
                SP.snap("finraw")
                SP.wait(s, "finraw")
            @blk.gpsimd
            def _(g):
                SP.wait(g, "finraw")

    if upto == "ph0":
        out_block()
        return nc
    ffn_gu_block(0)
    if upto == "gu":
        out_block()
        return nc
    ffn_down_block(0)
    if upto not in ("ffn1", "ffn1raw"):
        ln_block(0)
        proj_block()
        if upto == "qkvraw":
            raw_block()
            out_block()
            return nc
        if upto != "qkv":
            attn_block()
            if upto == "attnraw":
                raw_block()
                out_block()
                return nc
            if upto != "attn":
                ln_block(1)
                ffn_gu_block(1)
                ffn_down_block(1)
                if upto == "fullraw":
                    raw_block()
    out_block()
    return nc


def kernel(**inputs):
    consts = _host_constants()
    nc = build_nc("full")
    x = np.ascontiguousarray(np.asarray(inputs["L0_attn_x"]).reshape(L, D))
    base = {
        "Wq": inputs["L0_attn_Wq"], "Wk": inputs["L0_attn_Wk"], "Wv": inputs["L0_attn_Wv"],
        "bq": inputs["L0_attn_bq"], "bk": inputs["L0_attn_bk"], "bv": inputs["L0_attn_bv"],
        "bw": np.asarray(inputs["L0_attn_bias_weight"]).reshape(1, NH * 3),
        "Wg": inputs["L0_ffn_Wg"], "Wu": inputs["L0_ffn_Wu"], "Wd": inputs["L0_ffn_Wd"],
        "ag": inputs["L0_ffn_ag"], "au": inputs["L0_ffn_au"], "ad": inputs["L0_ffn_ad"],
    }
    base = {k: np.ascontiguousarray(np.asarray(v).astype(np.float32, copy=False)) for k, v in base.items()}
    base.update(consts)
    in_maps = [dict(base, x=np.ascontiguousarray(x[c * T:(c + 1) * T])) for c in range(NCORES)]
    res = run_bass_kernel_spmd(nc, in_maps, core_ids=list(range(NCORES)))
    o = np.concatenate([np.asarray(r["out"]) for r in res.results], axis=0)
    return o.reshape(1, L, D)


if __name__ == "__main__":
    import sys
    nc = build_nc(sys.argv[1] if len(sys.argv) > 1 else "full")
    n = sum(1 for _ in nc.all_instructions())
    print("built; instructions:", n)
